# version 31
# baseline (speedup 1.0000x reference)
import contextlib

import numpy as np
import ml_dtypes

import concourse.bass as bass
import concourse.mybir as mybir
from concourse.bass_utils import run_bass_kernel_spmd

F32 = mybir.dt.float32
BF16 = mybir.dt.bfloat16
AF = mybir.ActivationFunctionType
ALU = mybir.AluOpType

S = 4096
D = 1024
HF = 2048
EXPERTS = [1]
NE = len(EXPERTS)
NT = 32
NBLK = 8
NQ = 4


def _gates_f32():
    e = np.exp(np.float32([0.0, -100.0, -100.0, -100.0]))
    gk = (e / e.sum()).astype(np.float32)
    g = np.zeros(4, np.float32)
    g[1] = gk[0]
    g[0] = gk[1]
    g[2] = gk[2]
    g[3] = gk[3]
    return g[EXPERTS]


class _Sem:
    __slots__ = ("h", "n", "name")

    def __init__(self, h, name):
        self.h = h
        self.n = 0
        self.name = name


class _B:

    ENGS = ("sync", "gpsimd", "tensor", "scalar", "vector")

    def __init__(self, nc):
        self.nc = nc
        self.q = {e: [] for e in self.ENGS}
        self.wm = {e: {} for e in self.ENGS}
        self.sems = []

    def sem(self, stack, name):
        h = stack.enter_context(self.nc.semaphore(name))
        s = _Sem(h, name)
        self.sems.append(s)
        return s

    def wait(self, eng, sem, val):
        if not val or val <= 0:
            return
        if self.wm[eng].get(sem.name, 0) >= val:
            return
        self.wm[eng][sem.name] = val
        self.q[eng].append(lambda e, _s=sem.h, _v=val: e.wait_ge(_s, _v))

    def op(self, eng, fn, inc=None, k=1):
        if inc is None:
            self.q[eng].append(fn)
            return None
        inc.n += k

        def run(e, _f=fn, _s=inc.h, _k=k):
            _f(e).then_inc(_s, _k)

        self.q[eng].append(run)
        return inc.n

    def vspacer(self):
        self.q["vector"].append(lambda e, _j=self.vjunk: e.tensor_copy(_j[:], _j[:]))

    def barrier(self):
        for eng in self.ENGS:
            for s in self.sems:
                self.wait(eng, s, s.n)

    def replay(self, block):
        for eng in self.ENGS:
            ops = self.q[eng]

            def body(h, _ops=ops):
                for f in _ops:
                    f(h)

            getattr(block, eng)(body)


class _LN:

    RING = 4

    def __init__(self, b, stack, nc, epsc):
        self.b = b
        self.epsc = epsc
        self.sets = []
        for i in range(self.RING):
            s = {}
            for nm, w in (("b6", 12), ("sq", 12), ("h1", 6), ("hs", 6), ("s3", 3),
                          ("q3", 3), ("nm", 1), ("n2", 1), ("v1", 1), ("va", 1), ("rs", 1)):
                s[nm] = stack.enter_context(nc.sbuf_tensor(f"ln_{nm}_{i}", [128, w], F32))
            s["norm_ve"] = 0
            self.sets.append(s)
        self.i = 0

    def emit(self, x_ap, out_ap, pre_vec_waits=()):
        b = self.b
        s = self.sets[self.i % self.RING]
        self.i += 1
        sc, ve = b.sc, b.ve
        for sem, val in pre_vec_waits:
            b.wait("vector", sem, val)
        b.vspacer()
        b.op("vector", lambda e, a=s["b6"][:, 0:6], x=x_ap[:, 0:512]: e.bn_stats(a, x))
        b.op("vector", lambda e, a=s["b6"][:, 6:12], x=x_ap[:, 512:1024]: e.bn_stats(a, x))
        b.vspacer()
        b.op("vector", lambda e, o=s["h1"][:], i0=s["b6"][:, 0:6], i1=s["b6"][:, 6:12]:
             e.tensor_add(o, i0, i1))
        b.op("vector", lambda e, o=s["sq"][:], i0=s["b6"][:]: e.tensor_mul(o, i0, i0))
        b.op("vector", lambda e, o=s["s3"][:], i0=s["h1"][:, 0:3], i1=s["h1"][:, 3:6]:
             e.tensor_add(o, i0, i1))
        b.op("vector", lambda e, o=s["hs"][:], i0=s["sq"][:, 0:6], i1=s["sq"][:, 6:12]:
             e.tensor_add(o, i0, i1))
        b.op("vector", lambda e, o=s["nm"][:], i=s["s3"][:, 1:2]: e.tensor_scalar_mul(o, i, -0.25))
        b.op("vector", lambda e, o=s["q3"][:], i0=s["hs"][:, 0:3], i1=s["hs"][:, 3:6]:
             e.tensor_add(o, i0, i1))
        b.op("vector", lambda e, o=s["n2"][:], i0=s["nm"][:]: e.tensor_mul(o, i0, i0))
        b.vspacer()
        b.op("vector", lambda e, o=s["v1"][:], i0=s["q3"][:, 1:2], i1=s["n2"][:]:
             e.scalar_tensor_tensor(o, i0, 0.25, i1, ALU.mult, ALU.subtract))
        b.vspacer()
        var_ve = b.op("vector", lambda e, o=s["va"][:], i0=s["s3"][:, 2:3], i1=s["v1"][:]:
                      e.scalar_tensor_tensor(o, i0, 1.0 / 1024.0, i1, ALU.mult, ALU.add),
                      inc=ve)
        b.wait("scalar", ve, max(var_ve, s["norm_ve"]))
        b.op("scalar", lambda e, o=s["rs"][:], i=s["va"][:], bb=self.epsc[:]:
             e.activation(o, i, AF.Ln, bias=bb))
        rs_sc = b.op("scalar", lambda e, o=s["rs"][:]: e.activation(o, o, AF.Exp, scale=-0.5),
                     inc=sc)
        b.wait("vector", sc, rs_sc)
        norm_ve = b.op("vector", lambda e, o=out_ap, x=x_ap, a=s["nm"][:], r=s["rs"][:]:
                       e.tensor_scalar(o, x, a, r, ALU.add, ALU.mult), inc=ve)
        s["norm_ve"] = norm_ve
        return norm_ve


def _build_program():
    nc = bass.Bass("TRN2", target_bir_lowering=False, debug=False)

    x_h = nc.dram_tensor("x", [S, D], F32, kind="ExternalInput").ap()
    wq_h = nc.dram_tensor("wq", [D, D], F32, kind="ExternalInput").ap()
    wk_h = nc.dram_tensor("wk", [D, D], F32, kind="ExternalInput").ap()
    wv_h = nc.dram_tensor("wv", [D, D], F32, kind="ExternalInput").ap()
    wo_h = nc.dram_tensor("wo", [D, D], F32, kind="ExternalInput").ap()
    bo_h = nc.dram_tensor("bo", [1, D], F32, kind="ExternalInput").ap()
    w1_h = nc.dram_tensor("w1", [NE, D, HF], F32, kind="ExternalInput").ap()
    w2_h = nc.dram_tensor("w2", [NE, HF, D], F32, kind="ExternalInput").ap()
    b1_h = nc.dram_tensor("b1", [NE * 16, 128], F32, kind="ExternalInput").ap()
    b2_h = nc.dram_tensor("b2", [NE, D], F32, kind="ExternalInput").ap()
    g_h = nc.dram_tensor("gcol", [NE, 1], F32, kind="ExternalInput").ap()
    idb_h = nc.dram_tensor("identb", [128, 128], BF16, kind="ExternalInput").ap()
    idf_h = nc.dram_tensor("identf", [128, 128], F32, kind="ExternalInput").ap()
    oc_h = nc.dram_tensor("onescol", [128, 1], BF16, kind="ExternalInput").ap()
    or_h = nc.dram_tensor("onesrow", [1, 128], F32, kind="ExternalInput").ap()
    orb_h = nc.dram_tensor("onesrowb", [1, 128], BF16, kind="ExternalInput").ap()
    of_h = nc.dram_tensor("onesf", [128, 64], F32, kind="ExternalInput").ap()
    tri_h = nc.dram_tensor("tri", [128, 128], BF16, kind="ExternalInput").ap()

    out_h = nc.dram_tensor("out", [S, D], F32, kind="ExternalOutput").ap()

    qt_h = nc.dram_tensor("qt_scr", [NBLK, 128, S], BF16, kind="Internal").ap()
    x1_h = nc.dram_tensor("x1_scr", [NT, 128, D], F32, kind="Internal").ap()
    w1b_h = nc.dram_tensor("w1b_scr", [NE, D, HF], BF16, kind="Internal").ap()
    w2b_h = nc.dram_tensor("w2b_scr", [NE, HF, D], BF16, kind="Internal").ap()

    GV = [float(g) for g in _gates_f32()]

    outer = contextlib.ExitStack()
    with outer:
        E = outer.enter_context
        ident_b = E(nc.sbuf_tensor("ident_b", [128, 128], BF16))
        onescol = E(nc.sbuf_tensor("onescol_s", [128, 1], BF16))
        onesrow = E(nc.sbuf_tensor("onesrow_s", [1, 128], F32))
        onesf = E(nc.sbuf_tensor("onesf_s", [128, 64], F32))
        tri = E(nc.sbuf_tensor("tri_s", [128, 128], BF16))
        epsc = E(nc.sbuf_tensor("epsc", [128, 1], F32))
        bo_rowb = E(nc.sbuf_tensor("bo_rowb", [1, D], BF16))
        c2rowb = E(nc.sbuf_tensor("c2rowb", [1, D], BF16))
        onesrowb = E(nc.sbuf_tensor("onesrowb_s", [1, 128], BF16))
        b1g = E(nc.sbuf_tensor("b1g", [128, NE * 16], F32))
        vjunk = E(nc.sbuf_tensor("vjunk", [128, 16], F32))

        b = _B(nc)
        b.vjunk = vjunk
        tn = b.tn = b.sem(outer, "tn")
        sc = b.sc = b.sem(outer, "sc")
        ve = b.ve = b.sem(outer, "ve")
        s_gp = b.sem(outer, "s_gp")
        s_ld = b.sem(outer, "s_ld")
        s_x = [b.sem(outer, "s_x0"), b.sem(outer, "s_x1")]
        s_qst = [b.sem(outer, "s_qst0"), b.sem(outer, "s_qst1")]
        s_qt = [b.sem(outer, "s_qt0"), b.sem(outer, "s_qt1")]
        s_xs = [b.sem(outer, "s_xs0"), b.sem(outer, "s_xs1")]
        s_x1 = [b.sem(outer, "s_x10"), b.sem(outer, "s_x11")]
        s_w1 = b.sem(outer, "s_w1")
        s_w2 = b.sem(outer, "s_w2")
        s_out = [b.sem(outer, "s_out0"), b.sem(outer, "s_out1")]

        ln = _LN(b, outer, nc, epsc)

        pp = contextlib.ExitStack()
        SC0 = pp.enter_context(nc.psum_tensor("SC0", [128, 1024], F32))
        SC1 = pp.enter_context(nc.psum_tensor("SC1", [128, 1024], F32))
        PZA = [pp.enter_context(nc.psum_tensor(f"PZA{i}", [128, 512], F32)) for i in range(2)]
        ROT = [SC0[:, 0:512], SC0[:, 512:1024], SC1[:, 0:512], SC1[:, 512:1024],
               PZA[0][:, :], PZA[1][:, :]]

        kv_stack = contextlib.ExitStack()
        kT = kv_stack.enter_context(nc.sbuf_tensor("kT_res", [128, S * 8], BF16))
        v_res = kv_stack.enter_context(nc.sbuf_tensor("v_res", [128, S * 8], BF16))

        p1 = contextlib.ExitStack()
        xr = [p1.enter_context(nc.sbuf_tensor(f"xr{i}", [128, D], F32)) for i in range(2)]
        ln1r = [p1.enter_context(nc.sbuf_tensor(f"ln1r{i}", [128, D], BF16)) for i in range(2)]
        lnT = p1.enter_context(nc.sbuf_tensor("lnT", [128, S], BF16))
        qtst = [p1.enter_context(nc.sbuf_tensor(f"qtst{i}", [128, 512], BF16)) for i in range(2)]
        p1ps = contextlib.ExitStack()
        T0 = [p1ps.enter_context(nc.psum_tensor(f"T0_{i}", [128, 1024], BF16))
              for i in range(2)]
        setup_st = contextlib.ExitStack()
        ident_f = setup_st.enter_context(nc.sbuf_tensor("ident_f", [128, 128], F32))
        b2sb = setup_st.enter_context(nc.sbuf_tensor("b2sb", [NE, D], F32))
        g_sb = setup_st.enter_context(nc.sbuf_tensor("g_sb", [NE, 1], F32))
        b1st = setup_st.enter_context(nc.sbuf_tensor("b1st", [NE * 16, 128], F32))

        for dst, src in ((ident_b, idb_h), (ident_f, idf_h), (onescol, oc_h),
                         (onesrow, or_h), (onesrowb, orb_h), (onesf, of_h), (tri, tri_h),
                         (b2sb, b2_h), (g_sb, g_h), (b1st, b1_h)):
            b.op("sync", lambda e, d=dst, s=src: e.dma_start(out=d[:], in_=s), inc=s_ld, k=16)
        b.op("gpsimd", lambda e: e.dma_start(out=bo_rowb[:], in_=bo_h), inc=s_ld, k=16)
        consts_ld = s_ld.n
        b.op("vector", lambda e: e.memset(epsc[:], 1e-5))

        b.wait("tensor", s_ld, consts_ld)
        t1 = None
        for dh in range(2):
            t1 = b.op("tensor", lambda e, dh=dh: e.matmul(
                SC1[0:1, 512 * dh:512 * dh + 512], g_sb[:, 0:1], b2sb[:, 512 * dh:512 * dh + 512],
                start=True, stop=True), inc=tn)
        b.wait("vector", tn, t1)
        b.op("vector", lambda e: e.tensor_copy(c2rowb[:], SC1[0:1, 0:1024]), inc=ve)
        t4 = b.op("tensor", lambda e: e.transpose(
            PZA[0][:, 0:NE * 16], b1st[:, :], ident_f[0:NE * 16, 0:NE * 16]), inc=tn)
        b.wait("vector", tn, t4)
        for i in range(NE):
            b.op("vector", lambda e, i=i: e.tensor_scalar_mul(
                b1g[:, 16 * i:16 * i + 16], PZA[0][:, 16 * i:16 * i + 16], GV[i]), inc=ve)
        setup_ve = ve.n
        setup_tn = tn.n
        setup_st.close()
        p1a = contextlib.ExitStack()
        wqsb = p1a.enter_context(nc.sbuf_tensor("wqsb", [128, D * 8], BF16))
        wksb = p1a.enter_context(nc.sbuf_tensor("wksb", [128, D * 8], BF16))
        wvsb = p1a.enter_context(nc.sbuf_tensor("wvsb", [128, D * 8], BF16))
        b.wait("gpsimd", tn, setup_tn)
        b.wait("gpsimd", ve, setup_ve)
        for j in range(8):
            b.op("gpsimd", lambda e, j=j: e.dma_start(
                out=wksb[:, 1024 * j:1024 * j + 1024], in_=wk_h[128 * j:128 * j + 128, :]),
                inc=s_gp, k=16)
        wk_gp = s_gp.n
        for j in range(8):
            b.op("gpsimd", lambda e, j=j: e.dma_start(
                out=wqsb[:, 1024 * j:1024 * j + 1024], in_=wq_h[128 * j:128 * j + 128, :]),
                inc=s_gp, k=16)
        wq_gp = s_gp.n
        for j in range(8):
            b.op("gpsimd", lambda e, j=j: e.dma_start(
                out=wvsb[:, 1024 * j:1024 * j + 1024], in_=wv_h[128 * j:128 * j + 128, :]),
                inc=s_gp, k=16)
        wv_gp = s_gp.n
        for i in range(NE):
            b.op("gpsimd", lambda e, i=i: e.dma_start(out=w1b_h[i], in_=w1_h[i]), inc=s_gp, k=16)
            b.op("gpsimd", lambda e, i=i: e.dma_start(out=w2b_h[i], in_=w2_h[i]), inc=s_gp, k=16)
        precast_gp = s_gp.n

        norm_ve = [0] * NT
        t0evac_ve = [0] * NT
        transp_tn = [0] * NT
        mmdone_tn = [0] * NBLK
        qst_war = [0, 0]
        rot_i = 0
        rot_ve = [(ve, setup_ve)] * 6

        def mm_group8(lhsT_fn, rhs_fn, bank):
            g_tn = None
            for k in range(8):
                fn = lambda e, k=k, bk=bank, lf=lhsT_fn, rf=rhs_fn: e.matmul(
                    ROT[bk], lf(k), rf(k), start=(k == 0), stop=(k == 7))
                if k == 7:
                    g_tn = b.op("tensor", fn, inc=tn)
                else:
                    b.op("tensor", fn)
            return g_tn

        def p1_ln(t):
            slot = t % 2
            if t >= 2:
                b.wait("sync", ve, norm_ve[t - 2])
            b.op("sync", lambda e, t=t, sl=slot: e.dma_start(
                out=xr[sl][:], in_=x_h[128 * t:128 * t + 128, :]), inc=s_x[slot], k=16)
            waits = [(s_x[slot], s_x[slot].n)]
            if t >= 2:
                waits.append((tn, transp_tn[t - 2]))
            norm_ve[t] = ln.emit(xr[slot][:], ln1r[slot][:], pre_vec_waits=waits)

        def p1_te(t):
            blk = t // 4
            a = t % 4
            b.wait("tensor", ve, norm_ve[t])
            if t >= 2:
                b.wait("tensor", sc, t0evac_ve[t - 2])
            for j in range(8):
                fn = lambda e, j=j, sl=t % 2: e.transpose(
                    T0[t % 2][:, 128 * j:128 * j + 128], ln1r[sl][:, 128 * j:128 * j + 128],
                    ident_b[:, :])
                if j == 7:
                    transp_tn[t] = b.op("tensor", fn, inc=tn)
                else:
                    b.op("tensor", fn)
            b.wait("scalar", tn, transp_tn[t])
            if blk >= 1:
                b.wait("scalar", tn, mmdone_tn[blk - 1])
            src3 = T0[t % 2].ap().rearrange("p (j c) -> p j c", j=8)
            dst3 = lnT.ap().rearrange("p (j s) -> p j s", j=8)[:, :, 128 * a:128 * a + 128]
            t0evac_ve[t] = b.op("scalar", lambda e, s3=src3, d3=dst3: e.copy(d3, s3),
                                inc=sc)

        p1_ln(0)
        p1_ln(1)
        for blk in range(NBLK):
            for a in range(4):
                t = 4 * blk + a
                p1_te(t)
                if t + 2 < NT:
                    p1_ln(t + 2)

            b.wait("tensor", sc, t0evac_ve[4 * blk + 3])
            b.wait("tensor", s_gp, wk_gp)
            for m in range(8):
                bank = rot_i % 6
                rot_i += 1
                b.wait("tensor", rot_ve[bank][0], rot_ve[bank][1])
                g_tn = mm_group8(
                    lambda k, m=m: wksb[:, 1024 * k + 128 * m:1024 * k + 128 * m + 128],
                    lambda k, blk=blk: lnT[:, 512 * k:512 * k + 512], bank)
                b.wait("scalar", tn, g_tn)
                rot_ve[bank] = (sc, b.op("scalar", lambda e, m=m, blk=blk, bk=bank: e.copy(
                    kT[:, S * m + 512 * blk:S * m + 512 * blk + 512], ROT[bk]), inc=sc))
            b.wait("tensor", s_gp, wq_gp)
            for m in range(8):
                bank = rot_i % 6
                rot_i += 1
                b.wait("tensor", rot_ve[bank][0], rot_ve[bank][1])
                g_tn = mm_group8(
                    lambda k, m=m: wqsb[:, 1024 * k + 128 * m:1024 * k + 128 * m + 128],
                    lambda k, blk=blk: lnT[:, 512 * k:512 * k + 512], bank)
                b.wait("vector", tn, g_tn)
                b.wait("vector", s_qst[m % 2], qst_war[m % 2])
                q_ve = b.op("vector", lambda e, m=m, bk=bank: e.tensor_scalar_mul(
                    qtst[m % 2][:], ROT[bk], 0.125), inc=ve)
                rot_ve[bank] = (ve, q_ve)
                b.wait("sync", ve, q_ve)
                qst_war[m % 2] = b.op("sync", lambda e, m=m, blk=blk: e.dma_start(
                    out=qt_h[blk][:, 512 * m:512 * m + 512], in_=qtst[m % 2][:]),
                    inc=s_qst[m % 2], k=16)
            b.wait("tensor", s_gp, wv_gp)
            for a in range(4):
                for h2 in range(2):
                    bank = rot_i % 6
                    rot_i += 1
                    b.wait("tensor", rot_ve[bank][0], rot_ve[bank][1])
                    g_tn = mm_group8(
                        lambda k, a=a, blk=blk: lnT[:, 512 * k + 128 * a:512 * k + 128 * a + 128],
                        lambda k, h2=h2: wvsb[:, 1024 * k + 512 * h2:1024 * k + 512 * h2 + 512],
                        bank)
                    b.wait("vector", tn, g_tn)
                    t_ = 4 * blk + a
                    rot_ve[bank] = (ve, b.op("vector",
                                             lambda e, t_=t_, h2=h2, bk=bank: e.tensor_copy(
                                                 v_res[:, 1024 * t_ + 512 * h2:1024 * t_ + 512 * h2 + 512],
                                                 ROT[bk]), inc=ve))
            mmdone_tn[blk] = tn.n

        p1a.close()
        p1ps.close()
        p1.close()
        b.barrier()
        p2 = contextlib.ExitStack()
        wosb = p2.enter_context(nc.sbuf_tensor("wosb", [128, D * 8], BF16))
        qtr = [p2.enter_context(nc.sbuf_tensor(f"qtr{i}", [128, S], BF16)) for i in range(2)]
        expT = [p2.enter_context(nc.sbuf_tensor(f"expT{i}", [128, 1024], BF16)) for i in range(2)]
        zs = [p2.enter_context(nc.sbuf_tensor(f"zs{i}", [128, S], BF16)) for i in range(2)]
        lrows = p2.enter_context(nc.sbuf_tensor("lrows", [128, 512], F32))
        recipb = p2.enter_context(nc.sbuf_tensor("recipb", [128, 512], F32))
        attn_tmp = p2.enter_context(nc.sbuf_tensor("attn_tmp", [128, D], F32))
        x1st = [p2.enter_context(nc.sbuf_tensor(f"x1st{i}", [128, D], F32)) for i in range(2)]
        bo_bc = p2.enter_context(nc.sbuf_tensor("bo_bc", [128, D], F32))
        p2ps = contextlib.ExitStack()
        PZB = [p2ps.enter_context(nc.psum_tensor(f"PZB{i}", [128, 512], F32)) for i in range(2)]
        bb_t = None
        for dh in range(2):
            bb_t = b.op("tensor", lambda e, dh=dh: e.matmul(
                PZB[dh][:, :], onesrowb[0:1, :], bo_rowb[0:1, 512 * dh:512 * dh + 512],
                start=True, stop=True), inc=tn)
        b.wait("vector", tn, bb_t)
        b.op("vector", lambda e: e.tensor_copy(bo_bc[:, 0:512], PZB[0][:, :]))
        bo_bc_ve = b.op("vector", lambda e: e.tensor_copy(bo_bc[:, 512:1024], PZB[1][:, :]),
                        inc=ve)

        for j in range(8):
            b.op("gpsimd", lambda e, j=j: e.dma_start(
                out=wosb[:, 1024 * j:1024 * j + 1024], in_=wo_h[128 * j:128 * j + 128, :]),
                inc=s_gp, k=16)
        wo_gp = s_gp.n
        b.wait("tensor", s_gp, wo_gp)

        SCp = [SC0, SC1]
        OPBANKS = [SC0[:, 0:512], SC0[:, 512:1024], SC1[:, 0:512], SC1[:, 512:1024]]
        exp_on_slot = [0, 0]
        expT_war = [0, 0]
        opevac = [0, 0, 0, 0]
        mul_ve = {}
        bcast_tn = 0
        zs_done = [0, 0]
        zs_war = [0, 0]
        x1add_ve = [0, 0]
        x1st_war = [0, 0]
        qld_war = [0, 0]
        pend_fin = None

        def xload(Q, a):
            sl = a % 2
            b.wait("sync", s_x1[sl], x1st_war[sl])
            b.wait("sync", ve, x1add_ve[sl])
            b.op("sync", lambda e, Q=Q, a=a, sl=sl: e.dma_start(
                out=x1st[sl][:], in_=x_h[512 * Q + 128 * a:512 * Q + 128 * a + 128, :]),
                inc=s_xs[sl], k=16)
            return s_xs[sl].n

        xs_cnt = {}

        def opgroup(Q, a, dh):
            g = 2 * a + dh
            bk = g % 4
            b.wait("tensor", ve, zs_done[Q % 2])
            b.wait("tensor", ve, opevac[bk])
            b.wait("tensor", sc, exp_on_slot[bk // 2])
            op_tn = None
            for p in range(8):
                fn = lambda e, a=a, dh=dh, p=p, bk=bk, Q=Q: e.matmul(
                    OPBANKS[bk], zs[Q % 2][:, 512 * p + 128 * a:512 * p + 128 * a + 128],
                    wosb[:, 1024 * p + 512 * dh:1024 * p + 512 * dh + 512],
                    start=(p == 0), stop=(p == 7), skip_group_check=True)
                if p == 7:
                    op_tn = b.op("tensor", fn, inc=tn)
                else:
                    b.op("tensor", fn)
            zs_war[Q % 2] = op_tn
            b.wait("vector", tn, op_tn)
            b.wait("vector", ve, bo_bc_ve)
            b.wait("vector", ve, x1add_ve[(a - 1) % 2])
            opevac[bk] = b.op("vector", lambda e, dh=dh, bk=bk: e.tensor_add(
                attn_tmp[:, 512 * dh:512 * dh + 512], OPBANKS[bk],
                bo_bc[:, 512 * dh:512 * dh + 512]), inc=ve)

        def opchain(Q, a):
            ln.emit(attn_tmp[:], attn_tmp[:],
                    pre_vec_waits=[(ve, opevac[(2 * a) % 4]), (ve, opevac[(2 * a + 1) % 4])])
            sl = a % 2
            b.wait("vector", s_xs[sl], xs_cnt[(Q, a)])
            b.vspacer()
            x1add_ve[sl] = b.op("vector", lambda e, sl=sl: e.tensor_add(
                x1st[sl][:], x1st[sl][:], attn_tmp[:]), inc=ve)
            b.wait("sync", ve, x1add_ve[sl])
            x1st_war[sl] = b.op("sync", lambda e, Q=Q, a=a, sl=sl: e.dma_start(
                out=x1_h[4 * Q + a], in_=x1st[sl][:]), inc=s_x1[sl], k=16)

        qt_cnts = {}
        b.wait("sync", s_qst[0], qst_war[0])
        b.wait("sync", s_qst[1], qst_war[1])
        b.op("sync", lambda e: e.dma_start(out=qtr[0][:], in_=qt_h[0]), inc=s_qt[0], k=16)
        qt_cnts[0] = s_qt[0].n

        for qb in range(NBLK + 1):
            if qb + 1 < NBLK:
                sl1 = (qb + 1) % 2
                if qb >= 1:
                    b.wait("sync", tn, qld_war[sl1])
                b.op("sync", lambda e, qb=qb, sl1=sl1: e.dma_start(
                    out=qtr[sl1][:], in_=qt_h[qb + 1]), inc=s_qt[sl1], k=16)
                qt_cnts[qb + 1] = s_qt[sl1].n
            qt_cnt = qt_cnts.get(qb, 0)

            Q = qb - 1
            pieces = [[] for _ in range(8)]
            if Q >= 0:
                def mkx(a, Q=Q):
                    def f():
                        xs_cnt[(Q, a)] = xload(Q, a)
                    return f

                def mkg(a, dh, Q=Q):
                    return lambda: opgroup(Q, a, dh)

                def mkc(a, Q=Q):
                    return lambda: opchain(Q, a)

                pieces[0] = [mkx(0)]
                pieces[1] = [mkg(0, 0), mkx(1)]
                pieces[2] = [mkg(0, 1), mkc(0)]
                pieces[3] = [mkg(1, 0), mkx(2)]
                pieces[4] = [mkg(1, 1), mkc(1)]
                pieces[5] = [mkg(2, 0), mkx(3)]
                pieces[6] = [mkg(2, 1), mkc(2)]
                pieces[7] = [mkg(3, 0), mkg(3, 1), mkc(3)]

            if qb == NBLK:
                for ps in pieces:
                    for f in ps:
                        f()
                break

            nch = 4 * (qb + 1)
            for pj in range(8):
                for f in pieces[pj]:
                    f()
                p2i = pj % 2

                def scores(c, pj=pj, qb=qb, qt_cnt=qt_cnt):
                    s2 = c % 2
                    b.wait("tensor", sc, exp_on_slot[s2])
                    b.wait("tensor", ve, opevac[2 * s2])
                    b.wait("tensor", ve, opevac[2 * s2 + 1])
                    b.wait("tensor", s_qt[qb % 2], qt_cnt)
                    b.op("tensor", lambda e, c=c, pj=pj, s2=s2, qb=qb: e.matmul(
                        SCp[s2][:, 0:512], kT[0:64, S * pj + 128 * c:S * pj + 128 * c + 128],
                        qtr[qb % 2][0:64, 512 * pj:512 * pj + 512],
                        start=True, stop=True, tile_position=(0, 0), skip_group_check=True))
                    return b.op("tensor", lambda e, c=c, pj=pj, s2=s2, qb=qb: e.matmul(
                        SCp[s2][:, 512:1024], kT[64:128, S * pj + 128 * c:S * pj + 128 * c + 128],
                        qtr[qb % 2][64:128, 512 * pj:512 * pj + 512],
                        start=True, stop=True, tile_position=(64, 0), skip_group_check=True),
                        inc=tn)

                sc_tn = {0: scores(0)}
                if nch > 1:
                    sc_tn[1] = scores(1)
                if pend_fin is not None:
                    pend_fin()
                    pend_fin = None
                for c in range(nch):
                    s2 = c % 2
                    b.wait("scalar", tn, sc_tn[c])
                    b.wait("scalar", tn, expT_war[c % 2])
                    exp_sc = b.op("scalar", lambda e, c=c, s2=s2: e.activation(
                        expT[c % 2][:], SCp[s2][:, 0:1024], AF.Exp), inc=sc)
                    exp_on_slot[s2] = exp_sc
                    jj = c - 4 * qb
                    mask_ve = None
                    if jj >= 0:
                        b.wait("vector", sc, exp_sc)
                        for half in (0, 512):
                            if jj > 0:
                                b.op("vector", lambda e, c=c, h=half, jj=jj: e.memset(
                                    expT[c % 2][:, h:h + 128 * jj], 0.0))
                            mask_ve = b.op("vector", lambda e, c=c, h=half, jj=jj: e.tensor_mul(
                                expT[c % 2][:, h + 128 * jj:h + 128 * jj + 128],
                                expT[c % 2][:, h + 128 * jj:h + 128 * jj + 128],
                                tri[:, :]), inc=ve)
                    if c + 2 < nch:
                        sc_tn[c + 2] = scores(c + 2)
                    b.wait("tensor", sc, exp_sc)
                    if mask_ve is not None:
                        b.wait("tensor", ve, mask_ve)
                    st_ = c == 0
                    sp_ = c == nch - 1
                    if st_:
                        b.wait("tensor", ve, bo_bc_ve)
                        idx2 = qb * 8 + pj - 2
                        if idx2 >= 0:
                            b.wait("tensor", ve, mul_ve[idx2])
                    b.op("tensor", lambda e, c=c, pj=pj, st_=st_, sp_=sp_, p2i=p2i: e.matmul(
                        PZA[p2i][0:64, :], v_res[:, 1024 * c + 128 * pj:1024 * c + 128 * pj + 64],
                        expT[c % 2][:, 0:512], start=st_, stop=sp_,
                        tile_position=(0, 0), skip_group_check=True))
                    b.op("tensor", lambda e, c=c, pj=pj, st_=st_, sp_=sp_, p2i=p2i: e.matmul(
                        PZA[p2i][64:128, :], v_res[:, 1024 * c + 128 * pj + 64:1024 * c + 128 * pj + 128],
                        expT[c % 2][:, 512:1024], start=st_, stop=sp_,
                        tile_position=(0, 64), skip_group_check=True))
                    b.op("tensor", lambda e, c=c, pj=pj, st_=st_, sp_=sp_, p2i=p2i: e.matmul(
                        PZB[p2i][64:65, :], onescol[:, 0:1],
                        expT[c % 2][:, 0:512], start=st_, stop=sp_,
                        tile_position=(0, 64), skip_group_check=True))
                    expT_war[c % 2] = b.op("tensor", lambda e, c=c, st_=st_, sp_=sp_, p2i=p2i: e.matmul(
                        PZB[p2i][32:33, :], onescol[:, 0:1],
                        expT[c % 2][:, 512:1024], start=st_, stop=sp_,
                        tile_position=(0, 32), skip_group_check=True), inc=tn)
                zl_last = expT_war[(nch - 1) % 2]

                b.wait("vector", tn, zl_last)
                b.wait("vector", tn, bcast_tn)
                b.op("vector", lambda e, p2i=p2i: e.tensor_copy(lrows[64:65, :], PZB[p2i][64:65, :]))
                cp_sc = b.op("vector", lambda e, p2i=p2i: e.tensor_copy(
                    lrows[32:33, :], PZB[p2i][32:33, :]), inc=ve)

                def fin(qb=qb, pj=pj, p2i=p2i, cp_sc=cp_sc):
                    nonlocal bcast_tn
                    b.wait("tensor", ve, cp_sc)
                    b.op("tensor", lambda e, p2i=p2i: e.matmul(
                        PZB[p2i][0:64, :], onesf[64:65, 0:64], lrows[64:65, :],
                        start=True, stop=True, tile_position=(64, 0), skip_group_check=True))
                    bcast_tn = b.op("tensor", lambda e, p2i=p2i: e.matmul(
                        PZB[p2i][64:128, :], onesf[32:33, 0:64], lrows[32:33, :],
                        start=True, stop=True, tile_position=(32, 64), skip_group_check=True),
                        inc=tn)
                    b.wait("vector", tn, bcast_tn)
                    b.op("vector", lambda e, p2i=p2i: e.reciprocal(recipb[:], PZB[p2i][:, :]))
                    b.vspacer()
                    b.wait("vector", tn, zs_war[qb % 2])
                    mul_ve[qb * 8 + pj] = b.op("vector", lambda e, pj=pj, qb=qb, p2i=p2i:
                                               e.tensor_mul(
                                                   zs[qb % 2][:, 512 * pj:512 * pj + 512],
                                                   PZA[p2i][:, :], recipb[:]), inc=ve)

                pend_fin = fin
            pend_fin()
            pend_fin = None
            zs_done[qb % 2] = mul_ve[qb * 8 + 7]
            qld_war[qb % 2] = tn.n

        p2ps.close()
        p2.close()
        kv_stack.close()
        b.barrier()
        ffn = contextlib.ExitStack()
        lnS = [ffn.enter_context(nc.sbuf_tensor(f"lnS{i}", [128, 4096], BF16))
               for i in range(2)]
        w1sb = ffn.enter_context(nc.sbuf_tensor("w1sb", [128, 16384], BF16))
        w2sb = ffn.enter_context(nc.sbuf_tensor("w2sb", [128, 16384], BF16))
        hT = [ffn.enter_context(nc.sbuf_tensor(f"hT{i}", [128, 8192], BF16)) for i in range(2)]
        acc2 = [ffn.enter_context(nc.sbuf_tensor(f"acc2{i}", [128, D], F32)) for i in range(2)]
        ln2r = [ffn.enter_context(nc.sbuf_tensor(f"ln2r{i}", [128, D], BF16)) for i in range(2)]
        xr3 = [ffn.enter_context(nc.sbuf_tensor(f"xr3{i}", [128, D], F32)) for i in range(2)]
        xr4 = [ffn.enter_context(nc.sbuf_tensor(f"xr4{i}", [128, D], F32)) for i in range(2)]
        outst = [ffn.enter_context(nc.sbuf_tensor(f"outst{i}", [128, D], F32)) for i in range(2)]
        c2_bc = ffn.enter_context(nc.sbuf_tensor("c2_bc", [128, D], F32))
        fps = contextlib.ExitStack()
        T0f = fps.enter_context(nc.psum_tensor("T0f", [128, 1024], BF16))
        PZBf = fps.enter_context(nc.psum_tensor("PZBf", [128, 512], F32))

        cb_t = b.op("tensor", lambda e: e.matmul(
            PZBf[:, :], onesrowb[0:1, :], c2rowb[0:1, 0:512],
            start=True, stop=True), inc=tn)
        b.wait("vector", tn, cb_t)
        cb_v = b.op("vector", lambda e: e.tensor_copy(c2_bc[:, 0:512], PZBf[:, :]), inc=ve)
        b.wait("tensor", ve, cb_v)
        cb_t = b.op("tensor", lambda e: e.matmul(
            PZBf[:, :], onesrowb[0:1, :], c2rowb[0:1, 512:1024],
            start=True, stop=True), inc=tn)
        b.wait("vector", tn, cb_t)
        c2_bc_ve = b.op("vector", lambda e: e.tensor_copy(c2_bc[:, 512:1024], PZBf[:, :]),
                        inc=ve)
        b.wait("tensor", ve, c2_bc_ve)

        w2v = w2b_h.rearrange("E (k g p) n -> E k p g n", g=2, p=128)
        b.wait("sync", s_gp, precast_gp)
        for j in range(8):
            b.op("sync", lambda e, j=j: e.dma_start(
                out=w1sb[:, 2048 * j:2048 * j + 2048],
                in_=w1b_h[0][128 * j:128 * j + 128, :]), inc=s_w1, k=16)
        w1_cnt = s_w1.n
        for j in range(8):
            dst = w2sb.ap()[:, 2048 * j:2048 * j + 2048].rearrange("p (g n) -> p g n", g=2)
            b.op("sync", lambda e, j=j, d=dst: e.dma_start(
                out=d, in_=w2v[0, j]), inc=s_w2, k=16)
        w2_cnt = s_w2.n

        NS = 8
        YB = [SC0, SC1]
        HB = [PZA[0][:, :], PZA[1][:, :], PZBf[:, :]]
        x1_full = [s_x1[0].n, s_x1[1].n]
        a_norm = {}
        a_tr = {}
        a_done = [0, 0]
        b_done = {}
        hT_war = [0, 0]
        hrot_sc = [0] * 3
        yevac_ve = [0, 0]
        acc_war = [0, 0]
        out_war = [0, 0]
        xr3_war = [0, 0]
        xr4_war = [0, 0]
        t0_ve = 0
        tr_tn = 0

        def a_ln(s, tt):
            t = 4 * s + tt
            sl = t % 2
            b.wait("sync", s_x1[0], x1_full[0])
            b.wait("sync", s_x1[1], x1_full[1])
            b.wait("sync", ve, xr4_war[sl])
            b.op("sync", lambda e, t=t, sl=sl: e.dma_start(
                out=xr4[sl][:], in_=x1_h[t]), inc=s_x[sl], k=16)
            waits = [(s_x[sl], s_x[sl].n)]
            if t >= 2:
                waits.append((tn, a_tr.get(t - 2, 0)))
            a_norm[t] = ln.emit(xr4[sl][:], ln2r[sl][:], pre_vec_waits=waits)
            xr4_war[sl] = a_norm[t]

        def a_te(s, tt):
            nonlocal t0_ve, tr_tn
            t = 4 * s + tt
            sl = t % 2
            ls = s % 2
            b.wait("tensor", ve, a_norm[t])
            b.wait("tensor", sc, t0_ve)
            b.wait("tensor", tn, b_done.get(s - 2, 0))
            for j in range(8):
                fn = lambda e, j=j, sl=sl: e.transpose(
                    T0f[:, 128 * j:128 * j + 128], ln2r[sl][:, 128 * j:128 * j + 128],
                    ident_b[:, :])
                if j == 7:
                    tr_tn = b.op("tensor", fn, inc=tn)
                else:
                    b.op("tensor", fn)
            a_tr[t] = tr_tn
            b.wait("scalar", tn, tr_tn)
            b.wait("scalar", tn, b_done.get(s - 2, 0))
            src3 = T0f.ap().rearrange("p (j c) -> p j c", j=8)
            dst3 = lnS[ls].ap().rearrange("p (j s) -> p j s", j=8)[:, :, 128 * tt:128 * tt + 128]
            t0_ve = b.op("scalar", lambda e, s3=src3, d3=dst3: e.copy(d3, s3), inc=sc)
            if tt == 3:
                a_done[ls] = t0_ve

        def stage_a(s):
            a_ln(s, 0)
            a_ln(s, 1)
            for tt in range(4):
                a_te(s, tt)
                if tt + 2 < 4:
                    a_ln(s, tt + 2)

        def stage_b(s):
            ls = s % 2
            hs = s % 2
            b.wait("tensor", sc, a_done[ls])
            b.wait("tensor", s_w1, w1_cnt)
            h_last = 0
            for m in range(16):
                hb = m % 3
                b.wait("tensor", sc, hrot_sc[hb])
                h_tn = None
                for k in range(8):
                    fn = lambda e, m=m, k=k, hb=hb, ls=ls: e.matmul(
                        HB[hb], w1sb[:, 2048 * k + 128 * m:2048 * k + 128 * m + 128],
                        lnS[ls][:, 512 * k:512 * k + 512],
                        start=(k == 0), stop=(k == 7))
                    if k == 7:
                        h_tn = b.op("tensor", fn, inc=tn)
                    else:
                        b.op("tensor", fn)
                h_last = h_tn
                b.wait("scalar", tn, h_tn)
                b.wait("scalar", tn, hT_war[hs])
                hrot_sc[hb] = b.op("scalar", lambda e, m=m, hs=hs, hb=hb: e.activation(
                    hT[hs][:, 512 * m:512 * m + 512], HB[hb], AF.Relu,
                    bias=b1g[:, m:m + 1], scale=GV[0]), inc=sc)
            b_done[s] = h_last
            return sc.n

        def stage_c(s, relu_done):
            hs = s % 2
            b.wait("tensor", sc, relu_done)
            b.wait("tensor", s_w2, w2_cnt)
            y_last = 0
            for a in range(4):
                t = 4 * s + a
                asl = a % 2
                yb = a % 2
                b.wait("tensor", ve, yevac_ve[yb])
                y_tn = None
                for m in range(16):
                    for dh in range(2):
                        fn = lambda e, m=m, dh=dh, a=a, hs=hs, yb=yb: e.matmul(
                            YB[yb][:, 512 * dh:512 * dh + 512],
                            hT[hs][:, 512 * m + 128 * a:512 * m + 128 * a + 128],
                            w2sb[:, 1024 * m + 512 * dh:1024 * m + 512 * dh + 512],
                            start=(m == 0), stop=(m == 15))
                        if m == 15 and dh == 1:
                            y_tn = b.op("tensor", fn, inc=tn)
                        else:
                            b.op("tensor", fn)
                y_last = y_tn
                b.wait("vector", tn, y_tn)
                b.wait("vector", ve, acc_war[asl])
                yevac_ve[yb] = b.op("vector", lambda e, yb=yb, asl=asl:
                                    e.scalar_tensor_tensor(
                                        acc2[asl][:], YB[yb][:, 0:1024], 1.0, c2_bc[:],
                                        ALU.mult, ALU.add), inc=ve)
                ln.emit(acc2[asl][:], acc2[asl][:],
                        pre_vec_waits=[(ve, yevac_ve[yb])])
                b.wait("sync", ve, xr3_war[asl])
                b.op("sync", lambda e, t=t, asl=asl: e.dma_start(
                    out=xr3[asl][:], in_=x1_h[t]), inc=s_xs[asl], k=16)
                b.wait("vector", s_xs[asl], s_xs[asl].n)
                b.wait("vector", s_out[asl], out_war[asl])
                b.vspacer()
                o_ve = b.op("vector", lambda e, asl=asl: e.tensor_add(
                    outst[asl][:], xr3[asl][:], acc2[asl][:]), inc=ve)
                xr3_war[asl] = o_ve
                acc_war[asl] = o_ve
                b.wait("gpsimd", ve, o_ve)
                out_war[asl] = b.op("gpsimd", lambda e, t=t, asl=asl: e.dma_start(
                    out=out_h[128 * t:128 * t + 128, :], in_=outst[asl][:]),
                    inc=s_out[asl], k=16)
            hT_war[hs] = y_last

        stage_a(0)
        stage_a(1)
        for s in range(NS):
            rd = stage_b(s)
            if s + 2 < NS:
                stage_a(s + 2)
            stage_c(s, rd)

        b.wait("gpsimd", s_out[0], s_out[0].n)
        b.wait("gpsimd", s_out[1], s_out[1].n)

        fps.close()
        ffn.close()
        pp.close()

        block = outer.enter_context(nc.Block())
        b.replay(block)
    return nc


def kernel(**inputs):
    x = np.ascontiguousarray(np.asarray(inputs["L0_attn_resid"], dtype=np.float32))
    wq = np.ascontiguousarray(np.asarray(inputs["L0_attn_W_Q"], np.float32).reshape(D, D))
    wk = np.ascontiguousarray(np.asarray(inputs["L0_attn_W_K"], np.float32).reshape(D, D))
    wv = np.ascontiguousarray(np.asarray(inputs["L0_attn_W_V"], np.float32).reshape(D, D))
    wo = np.ascontiguousarray(
        np.asarray(inputs["L0_attn_W_out"], np.float32).transpose(1, 0, 2).reshape(D, D))
    bo = np.ascontiguousarray(np.asarray(inputs["L0_attn_b_out"], np.float32).reshape(1, D))
    w1 = np.ascontiguousarray(np.asarray(inputs["L0_ffn_W1"], np.float32)[EXPERTS])
    w2 = np.ascontiguousarray(np.asarray(inputs["L0_ffn_W2"], np.float32)[EXPERTS])
    b1 = np.ascontiguousarray(
        np.asarray(inputs["L0_ffn_b1"], np.float32)[EXPERTS].reshape(NE * 16, 128))
    b2 = np.ascontiguousarray(np.asarray(inputs["L0_ffn_b2"], np.float32)[EXPERTS])

    gcol = _gates_f32().reshape(NE, 1)
    identb = np.eye(128, dtype=ml_dtypes.bfloat16)
    identf = np.eye(128, dtype=np.float32)
    onescol = np.ones((128, 1), ml_dtypes.bfloat16)
    onesrow = np.ones((1, 128), np.float32)
    onesf = np.ones((128, 64), np.float32)
    tri = np.triu(np.ones((128, 128), np.float32)).astype(ml_dtypes.bfloat16)

    nc = _build_program()
    shared = {"wq": wq, "wk": wk, "wv": wv, "wo": wo, "bo": bo, "w1": w1, "w2": w2,
              "b1": b1, "b2": b2, "gcol": gcol, "identb": identb, "identf": identf,
              "onescol": onescol, "onesrow": onesrow, "onesf": onesf, "tri": tri,
              "onesrowb": np.ones((1, 128), ml_dtypes.bfloat16)}
    in_maps = [dict(shared, x=x[i]) for i in range(8)]
    res = run_bass_kernel_spmd(nc, in_maps, core_ids=list(range(8)))
    return np.stack([np.asarray(r["out"], np.float32) for r in res.results], axis=0)


# version 32
# speedup vs baseline: 1.0479x; 1.0479x over previous
import contextlib

import numpy as np
import ml_dtypes

import concourse.bass as bass
import concourse.mybir as mybir
from concourse.bass_utils import run_bass_kernel_spmd

F32 = mybir.dt.float32
BF16 = mybir.dt.bfloat16
AF = mybir.ActivationFunctionType
ALU = mybir.AluOpType

S = 4096
D = 1024
HF = 2048
EXPERTS = [1]
NE = len(EXPERTS)
NT = 32
NBLK = 8
NQ = 4


def _gates_f32():
    e = np.exp(np.float32([0.0, -100.0, -100.0, -100.0]))
    gk = (e / e.sum()).astype(np.float32)
    g = np.zeros(4, np.float32)
    g[1] = gk[0]
    g[0] = gk[1]
    g[2] = gk[2]
    g[3] = gk[3]
    return g[EXPERTS]


class _Sem:
    __slots__ = ("h", "n", "name")

    def __init__(self, h, name):
        self.h = h
        self.n = 0
        self.name = name


class _B:

    ENGS = ("sync", "gpsimd", "tensor", "scalar", "vector")

    def __init__(self, nc):
        self.nc = nc
        self.q = {e: [] for e in self.ENGS}
        self.wm = {e: {} for e in self.ENGS}
        self.sems = []

    def sem(self, stack, name):
        h = stack.enter_context(self.nc.semaphore(name))
        s = _Sem(h, name)
        self.sems.append(s)
        return s

    def wait(self, eng, sem, val):
        if not val or val <= 0:
            return
        if self.wm[eng].get(sem.name, 0) >= val:
            return
        self.wm[eng][sem.name] = val
        self.q[eng].append(lambda e, _s=sem.h, _v=val: e.wait_ge(_s, _v))

    def op(self, eng, fn, inc=None, k=1):
        if inc is None:
            self.q[eng].append(fn)
            return None
        inc.n += k

        def run(e, _f=fn, _s=inc.h, _k=k):
            _f(e).then_inc(_s, _k)

        self.q[eng].append(run)
        return inc.n

    def vspacer(self):
        self.q["vector"].append(lambda e, _j=self.vjunk: e.tensor_copy(_j[:], _j[:]))

    def barrier(self):
        for eng in self.ENGS:
            for s in self.sems:
                self.wait(eng, s, s.n)

    def replay(self, block):
        for eng in self.ENGS:
            ops = self.q[eng]

            def body(h, _ops=ops):
                for f in _ops:
                    f(h)

            getattr(block, eng)(body)


class _LN:

    RING = 4

    def __init__(self, b, stack, nc, epsc):
        self.b = b
        self.epsc = epsc
        self.sets = []
        for i in range(self.RING):
            s = {}
            for nm, w in (("b6", 12), ("sq", 12), ("h1", 6), ("hs", 6), ("s3", 3),
                          ("q3", 3), ("nm", 1), ("n2", 1), ("v1", 1), ("va", 1), ("rs", 1)):
                s[nm] = stack.enter_context(nc.sbuf_tensor(f"ln_{nm}_{i}", [128, w], F32))
            s["norm_ve"] = 0
            self.sets.append(s)
        self.i = 0

    def emit(self, x_ap, out_ap, pre_vec_waits=()):
        b = self.b
        s = self.sets[self.i % self.RING]
        self.i += 1
        sc, ve = b.sc, b.ve
        for sem, val in pre_vec_waits:
            b.wait("vector", sem, val)
        b.vspacer()
        b.op("vector", lambda e, a=s["b6"][:, 0:6], x=x_ap[:, 0:512]: e.bn_stats(a, x))
        b.op("vector", lambda e, a=s["b6"][:, 6:12], x=x_ap[:, 512:1024]: e.bn_stats(a, x))
        b.vspacer()
        b.op("vector", lambda e, o=s["h1"][:], i0=s["b6"][:, 0:6], i1=s["b6"][:, 6:12]:
             e.tensor_add(o, i0, i1))
        b.op("vector", lambda e, o=s["sq"][:], i0=s["b6"][:]: e.tensor_mul(o, i0, i0))
        b.op("vector", lambda e, o=s["s3"][:], i0=s["h1"][:, 0:3], i1=s["h1"][:, 3:6]:
             e.tensor_add(o, i0, i1))
        b.op("vector", lambda e, o=s["hs"][:], i0=s["sq"][:, 0:6], i1=s["sq"][:, 6:12]:
             e.tensor_add(o, i0, i1))
        b.op("vector", lambda e, o=s["nm"][:], i=s["s3"][:, 1:2]: e.tensor_scalar_mul(o, i, -0.25))
        b.op("vector", lambda e, o=s["q3"][:], i0=s["hs"][:, 0:3], i1=s["hs"][:, 3:6]:
             e.tensor_add(o, i0, i1))
        b.op("vector", lambda e, o=s["n2"][:], i0=s["nm"][:]: e.tensor_mul(o, i0, i0))
        b.vspacer()
        b.op("vector", lambda e, o=s["v1"][:], i0=s["q3"][:, 1:2], i1=s["n2"][:]:
             e.scalar_tensor_tensor(o, i0, 0.25, i1, ALU.mult, ALU.subtract))
        b.vspacer()
        var_ve = b.op("vector", lambda e, o=s["va"][:], i0=s["s3"][:, 2:3], i1=s["v1"][:]:
                      e.scalar_tensor_tensor(o, i0, 1.0 / 1024.0, i1, ALU.mult, ALU.add),
                      inc=ve)
        b.wait("scalar", ve, max(var_ve, s["norm_ve"]))
        b.op("scalar", lambda e, o=s["rs"][:], i=s["va"][:], bb=self.epsc[:]:
             e.activation(o, i, AF.Ln, bias=bb))
        rs_sc = b.op("scalar", lambda e, o=s["rs"][:]: e.activation(o, o, AF.Exp, scale=-0.5),
                     inc=sc)
        b.wait("vector", sc, rs_sc)
        norm_ve = b.op("vector", lambda e, o=out_ap, x=x_ap, a=s["nm"][:], r=s["rs"][:]:
                       e.tensor_scalar(o, x, a, r, ALU.add, ALU.mult), inc=ve)
        s["norm_ve"] = norm_ve
        return norm_ve


def _build_program():
    nc = bass.Bass("TRN2", target_bir_lowering=False, debug=False)

    x_h = nc.dram_tensor("x", [S, D], F32, kind="ExternalInput").ap()
    wq_h = nc.dram_tensor("wq", [D, D], F32, kind="ExternalInput").ap()
    wk_h = nc.dram_tensor("wk", [D, D], F32, kind="ExternalInput").ap()
    wv_h = nc.dram_tensor("wv", [D, D], F32, kind="ExternalInput").ap()
    wo_h = nc.dram_tensor("wo", [D, D], F32, kind="ExternalInput").ap()
    bo_h = nc.dram_tensor("bo", [1, D], F32, kind="ExternalInput").ap()
    w1_h = nc.dram_tensor("w1", [NE, D, HF], F32, kind="ExternalInput").ap()
    w2_h = nc.dram_tensor("w2", [NE, HF, D], F32, kind="ExternalInput").ap()
    b1_h = nc.dram_tensor("b1", [NE * 16, 128], F32, kind="ExternalInput").ap()
    b2_h = nc.dram_tensor("b2", [NE, D], F32, kind="ExternalInput").ap()
    g_h = nc.dram_tensor("gcol", [NE, 1], F32, kind="ExternalInput").ap()
    idb_h = nc.dram_tensor("identb", [128, 128], BF16, kind="ExternalInput").ap()
    idf_h = nc.dram_tensor("identf", [128, 128], F32, kind="ExternalInput").ap()
    oc_h = nc.dram_tensor("onescol", [128, 1], BF16, kind="ExternalInput").ap()
    or_h = nc.dram_tensor("onesrow", [1, 128], F32, kind="ExternalInput").ap()
    orb_h = nc.dram_tensor("onesrowb", [1, 128], BF16, kind="ExternalInput").ap()
    of_h = nc.dram_tensor("onesf", [128, 64], F32, kind="ExternalInput").ap()
    tri_h = nc.dram_tensor("tri", [128, 128], BF16, kind="ExternalInput").ap()

    out_h = nc.dram_tensor("out", [S, D], F32, kind="ExternalOutput").ap()

    qt_h = nc.dram_tensor("qt_scr", [NBLK, 128, S], BF16, kind="Internal").ap()
    x1_h = nc.dram_tensor("x1_scr", [NT, 128, D], F32, kind="Internal").ap()
    w1b_h = nc.dram_tensor("w1b_scr", [NE, D, HF], BF16, kind="Internal").ap()
    w2b_h = nc.dram_tensor("w2b_scr", [NE, HF, D], BF16, kind="Internal").ap()

    GV = [float(g) for g in _gates_f32()]

    outer = contextlib.ExitStack()
    with outer:
        E = outer.enter_context
        ident_b = E(nc.sbuf_tensor("ident_b", [128, 128], BF16))
        onescol = E(nc.sbuf_tensor("onescol_s", [128, 1], BF16))
        onesrow = E(nc.sbuf_tensor("onesrow_s", [1, 128], F32))
        onesf = E(nc.sbuf_tensor("onesf_s", [128, 64], F32))
        tri = E(nc.sbuf_tensor("tri_s", [128, 128], BF16))
        epsc = E(nc.sbuf_tensor("epsc", [128, 1], F32))
        bo_rowb = E(nc.sbuf_tensor("bo_rowb", [1, D], BF16))
        c2rowb = E(nc.sbuf_tensor("c2rowb", [1, D], BF16))
        onesrowb = E(nc.sbuf_tensor("onesrowb_s", [1, 128], BF16))
        b1g = E(nc.sbuf_tensor("b1g", [128, NE * 16], F32))
        vjunk = E(nc.sbuf_tensor("vjunk", [128, 16], F32))

        b = _B(nc)
        b.vjunk = vjunk
        tn = b.tn = b.sem(outer, "tn")
        sc = b.sc = b.sem(outer, "sc")
        ve = b.ve = b.sem(outer, "ve")
        s_gp = b.sem(outer, "s_gp")
        s_ld = b.sem(outer, "s_ld")
        s_x = [b.sem(outer, "s_x0"), b.sem(outer, "s_x1")]
        s_qst = [b.sem(outer, "s_qst0"), b.sem(outer, "s_qst1")]
        s_qt = [b.sem(outer, "s_qt0"), b.sem(outer, "s_qt1")]
        s_xs = [b.sem(outer, "s_xs0"), b.sem(outer, "s_xs1")]
        s_x1 = [b.sem(outer, "s_x10"), b.sem(outer, "s_x11")]
        s_w1 = b.sem(outer, "s_w1")
        s_w2 = b.sem(outer, "s_w2")
        s_out = [b.sem(outer, "s_out0"), b.sem(outer, "s_out1")]

        ln = _LN(b, outer, nc, epsc)

        pp = contextlib.ExitStack()
        SC0 = pp.enter_context(nc.psum_tensor("SC0", [128, 1024], F32))
        SC1 = pp.enter_context(nc.psum_tensor("SC1", [128, 1024], F32))
        PZA = [pp.enter_context(nc.psum_tensor(f"PZA{i}", [128, 512], F32)) for i in range(2)]
        ROT = [SC0[:, 0:512], SC0[:, 512:1024], SC1[:, 0:512], SC1[:, 512:1024],
               PZA[0][:, :], PZA[1][:, :]]

        kv_stack = contextlib.ExitStack()
        kT = kv_stack.enter_context(nc.sbuf_tensor("kT_res", [128, S * 8], BF16))
        v_res = kv_stack.enter_context(nc.sbuf_tensor("v_res", [128, S * 8], BF16))

        p1 = contextlib.ExitStack()
        xr = [p1.enter_context(nc.sbuf_tensor(f"xr{i}", [128, D], F32)) for i in range(2)]
        ln1r = [p1.enter_context(nc.sbuf_tensor(f"ln1r{i}", [128, D], BF16)) for i in range(2)]
        lnT = p1.enter_context(nc.sbuf_tensor("lnT", [128, S], BF16))
        qtst = [p1.enter_context(nc.sbuf_tensor(f"qtst{i}", [128, 512], BF16)) for i in range(2)]
        p1ps = contextlib.ExitStack()
        T0 = [p1ps.enter_context(nc.psum_tensor(f"T0_{i}", [128, 1024], BF16))
              for i in range(2)]
        setup_st = contextlib.ExitStack()
        ident_f = setup_st.enter_context(nc.sbuf_tensor("ident_f", [128, 128], F32))
        b2sb = setup_st.enter_context(nc.sbuf_tensor("b2sb", [NE, D], F32))
        g_sb = setup_st.enter_context(nc.sbuf_tensor("g_sb", [NE, 1], F32))
        b1st = setup_st.enter_context(nc.sbuf_tensor("b1st", [NE * 16, 128], F32))

        for dst, src in ((ident_b, idb_h), (ident_f, idf_h), (onescol, oc_h),
                         (onesrow, or_h), (onesrowb, orb_h), (onesf, of_h), (tri, tri_h),
                         (b2sb, b2_h), (g_sb, g_h), (b1st, b1_h)):
            b.op("sync", lambda e, d=dst, s=src: e.dma_start(out=d[:], in_=s), inc=s_ld, k=16)
        b.op("gpsimd", lambda e: e.dma_start(out=bo_rowb[:], in_=bo_h), inc=s_ld, k=16)
        consts_ld = s_ld.n
        b.op("vector", lambda e: e.memset(epsc[:], 1e-5))

        b.wait("tensor", s_ld, consts_ld)
        t1 = None
        for dh in range(2):
            t1 = b.op("tensor", lambda e, dh=dh: e.matmul(
                SC1[0:1, 512 * dh:512 * dh + 512], g_sb[:, 0:1], b2sb[:, 512 * dh:512 * dh + 512],
                start=True, stop=True), inc=tn)
        b.wait("vector", tn, t1)
        b.op("vector", lambda e: e.tensor_copy(c2rowb[:], SC1[0:1, 0:1024]), inc=ve)
        t4 = b.op("tensor", lambda e: e.transpose(
            PZA[0][:, 0:NE * 16], b1st[:, :], ident_f[0:NE * 16, 0:NE * 16]), inc=tn)
        b.wait("vector", tn, t4)
        for i in range(NE):
            b.op("vector", lambda e, i=i: e.tensor_scalar_mul(
                b1g[:, 16 * i:16 * i + 16], PZA[0][:, 16 * i:16 * i + 16], GV[i]), inc=ve)
        setup_ve = ve.n
        setup_tn = tn.n
        setup_st.close()
        p1a = contextlib.ExitStack()
        wqsb = p1a.enter_context(nc.sbuf_tensor("wqsb", [128, D * 8], BF16))
        wksb = p1a.enter_context(nc.sbuf_tensor("wksb", [128, D * 8], BF16))
        wvsb = p1a.enter_context(nc.sbuf_tensor("wvsb", [128, D * 8], BF16))
        b.wait("gpsimd", tn, setup_tn)
        b.wait("gpsimd", ve, setup_ve)
        for j in range(8):
            b.op("gpsimd", lambda e, j=j: e.dma_start(
                out=wksb[:, 1024 * j:1024 * j + 1024], in_=wk_h[128 * j:128 * j + 128, :]),
                inc=s_gp, k=16)
        wk_gp = s_gp.n
        for j in range(8):
            b.op("gpsimd", lambda e, j=j: e.dma_start(
                out=wqsb[:, 1024 * j:1024 * j + 1024], in_=wq_h[128 * j:128 * j + 128, :]),
                inc=s_gp, k=16)
        wq_gp = s_gp.n
        for j in range(8):
            b.op("gpsimd", lambda e, j=j: e.dma_start(
                out=wvsb[:, 1024 * j:1024 * j + 1024], in_=wv_h[128 * j:128 * j + 128, :]),
                inc=s_gp, k=16)
        wv_gp = s_gp.n
        for i in range(NE):
            b.op("gpsimd", lambda e, i=i: e.dma_start(out=w1b_h[i], in_=w1_h[i]), inc=s_gp, k=16)
            b.op("gpsimd", lambda e, i=i: e.dma_start(out=w2b_h[i], in_=w2_h[i]), inc=s_gp, k=16)
        precast_gp = s_gp.n

        norm_ve = [0] * NT
        t0evac_ve = [0] * NT
        transp_tn = [0] * NT
        mmdone_tn = [0] * NBLK
        qst_war = [0, 0]
        rot_i = 0
        rot_ve = [(ve, setup_ve)] * 6

        def mm_group8(lhsT_fn, rhs_fn, bank):
            g_tn = None
            for k in range(8):
                fn = lambda e, k=k, bk=bank, lf=lhsT_fn, rf=rhs_fn: e.matmul(
                    ROT[bk], lf(k), rf(k), start=(k == 0), stop=(k == 7))
                if k == 7:
                    g_tn = b.op("tensor", fn, inc=tn)
                else:
                    b.op("tensor", fn)
            return g_tn

        def p1_ln(t):
            slot = t % 2
            if t >= 2:
                b.wait("sync", ve, norm_ve[t - 2])
            b.op("sync", lambda e, t=t, sl=slot: e.dma_start(
                out=xr[sl][:], in_=x_h[128 * t:128 * t + 128, :]), inc=s_x[slot], k=16)
            waits = [(s_x[slot], s_x[slot].n)]
            if t >= 2:
                waits.append((tn, transp_tn[t - 2]))
            norm_ve[t] = ln.emit(xr[slot][:], ln1r[slot][:], pre_vec_waits=waits)

        def p1_te(t):
            blk = t // 4
            a = t % 4
            b.wait("tensor", ve, norm_ve[t])
            if t >= 2:
                b.wait("tensor", ve, t0evac_ve[t - 2])
            for j in range(8):
                fn = lambda e, j=j, sl=t % 2: e.transpose(
                    T0[t % 2][:, 128 * j:128 * j + 128], ln1r[sl][:, 128 * j:128 * j + 128],
                    ident_b[:, :])
                if j == 7:
                    transp_tn[t] = b.op("tensor", fn, inc=tn)
                else:
                    b.op("tensor", fn)
            b.wait("vector", tn, transp_tn[t])
            if blk >= 1:
                b.wait("vector", tn, mmdone_tn[blk - 1])
            src3 = T0[t % 2].ap().rearrange("p (j c) -> p j c", j=8)
            dst3 = lnT.ap().rearrange("p (j s) -> p j s", j=8)[:, :, 128 * a:128 * a + 128]
            t0evac_ve[t] = b.op("vector", lambda e, s3=src3, d3=dst3: e.tensor_copy(d3, s3),
                                inc=ve)

        p1_ln(0)
        p1_ln(1)
        for blk in range(NBLK):
            for a in range(4):
                t = 4 * blk + a
                p1_te(t)
                if t + 2 < NT:
                    p1_ln(t + 2)

            b.wait("tensor", ve, t0evac_ve[4 * blk + 3])
            b.wait("tensor", s_gp, wk_gp)
            for m in range(8):
                bank = rot_i % 6
                rot_i += 1
                b.wait("tensor", rot_ve[bank][0], rot_ve[bank][1])
                g_tn = mm_group8(
                    lambda k, m=m: wksb[:, 1024 * k + 128 * m:1024 * k + 128 * m + 128],
                    lambda k, blk=blk: lnT[:, 512 * k:512 * k + 512], bank)
                b.wait("scalar", tn, g_tn)
                rot_ve[bank] = (sc, b.op("scalar", lambda e, m=m, blk=blk, bk=bank: e.copy(
                    kT[:, S * m + 512 * blk:S * m + 512 * blk + 512], ROT[bk]), inc=sc))
            b.wait("tensor", s_gp, wq_gp)
            for m in range(8):
                bank = rot_i % 6
                rot_i += 1
                b.wait("tensor", rot_ve[bank][0], rot_ve[bank][1])
                g_tn = mm_group8(
                    lambda k, m=m: wqsb[:, 1024 * k + 128 * m:1024 * k + 128 * m + 128],
                    lambda k, blk=blk: lnT[:, 512 * k:512 * k + 512], bank)
                b.wait("vector", tn, g_tn)
                b.wait("vector", s_qst[m % 2], qst_war[m % 2])
                q_ve = b.op("vector", lambda e, m=m, bk=bank: e.tensor_scalar_mul(
                    qtst[m % 2][:], ROT[bk], 0.125), inc=ve)
                rot_ve[bank] = (ve, q_ve)
                b.wait("sync", ve, q_ve)
                qst_war[m % 2] = b.op("sync", lambda e, m=m, blk=blk: e.dma_start(
                    out=qt_h[blk][:, 512 * m:512 * m + 512], in_=qtst[m % 2][:]),
                    inc=s_qst[m % 2], k=16)
            b.wait("tensor", s_gp, wv_gp)
            for a in range(4):
                for h2 in range(2):
                    bank = rot_i % 6
                    rot_i += 1
                    b.wait("tensor", rot_ve[bank][0], rot_ve[bank][1])
                    g_tn = mm_group8(
                        lambda k, a=a, blk=blk: lnT[:, 512 * k + 128 * a:512 * k + 128 * a + 128],
                        lambda k, h2=h2: wvsb[:, 1024 * k + 512 * h2:1024 * k + 512 * h2 + 512],
                        bank)
                    b.wait("vector", tn, g_tn)
                    t_ = 4 * blk + a
                    rot_ve[bank] = (ve, b.op("vector",
                                             lambda e, t_=t_, h2=h2, bk=bank: e.tensor_copy(
                                                 v_res[:, 1024 * t_ + 512 * h2:1024 * t_ + 512 * h2 + 512],
                                                 ROT[bk]), inc=ve))
            mmdone_tn[blk] = tn.n

        p1a.close()
        p1ps.close()
        p1.close()
        b.barrier()
        p2 = contextlib.ExitStack()
        wosb = p2.enter_context(nc.sbuf_tensor("wosb", [128, D * 8], BF16))
        qtr = [p2.enter_context(nc.sbuf_tensor(f"qtr{i}", [128, S], BF16)) for i in range(2)]
        expT = [p2.enter_context(nc.sbuf_tensor(f"expT{i}", [128, 1024], BF16)) for i in range(2)]
        zs = [p2.enter_context(nc.sbuf_tensor(f"zs{i}", [128, S], BF16)) for i in range(2)]
        lrows = p2.enter_context(nc.sbuf_tensor("lrows", [128, 512], F32))
        recipb = p2.enter_context(nc.sbuf_tensor("recipb", [128, 512], F32))
        attn_tmp = p2.enter_context(nc.sbuf_tensor("attn_tmp", [128, D], F32))
        x1st = [p2.enter_context(nc.sbuf_tensor(f"x1st{i}", [128, D], F32)) for i in range(2)]
        bo_bc = p2.enter_context(nc.sbuf_tensor("bo_bc", [128, D], F32))
        p2ps = contextlib.ExitStack()
        PZB = [p2ps.enter_context(nc.psum_tensor(f"PZB{i}", [128, 512], F32)) for i in range(2)]
        bb_t = None
        for dh in range(2):
            bb_t = b.op("tensor", lambda e, dh=dh: e.matmul(
                PZB[dh][:, :], onesrowb[0:1, :], bo_rowb[0:1, 512 * dh:512 * dh + 512],
                start=True, stop=True), inc=tn)
        b.wait("vector", tn, bb_t)
        b.op("vector", lambda e: e.tensor_copy(bo_bc[:, 0:512], PZB[0][:, :]))
        bo_bc_ve = b.op("vector", lambda e: e.tensor_copy(bo_bc[:, 512:1024], PZB[1][:, :]),
                        inc=ve)

        for j in range(8):
            b.op("gpsimd", lambda e, j=j: e.dma_start(
                out=wosb[:, 1024 * j:1024 * j + 1024], in_=wo_h[128 * j:128 * j + 128, :]),
                inc=s_gp, k=16)
        wo_gp = s_gp.n
        b.wait("tensor", s_gp, wo_gp)

        SCp = [SC0, SC1]
        OPBANKS = [SC0[:, 0:512], SC0[:, 512:1024], SC1[:, 0:512], SC1[:, 512:1024]]
        exp_on_slot = [0, 0]
        expT_war = [0, 0]
        opevac = [0, 0, 0, 0]
        mul_ve = {}
        bcast_tn = 0
        zs_done = [0, 0]
        zs_war = [0, 0]
        x1add_ve = [0, 0]
        x1st_war = [0, 0]
        qld_war = [0, 0]
        pend_fin = None

        def xload(Q, a):
            sl = a % 2
            b.wait("sync", s_x1[sl], x1st_war[sl])
            b.wait("sync", ve, x1add_ve[sl])
            b.op("sync", lambda e, Q=Q, a=a, sl=sl: e.dma_start(
                out=x1st[sl][:], in_=x_h[512 * Q + 128 * a:512 * Q + 128 * a + 128, :]),
                inc=s_xs[sl], k=16)
            return s_xs[sl].n

        xs_cnt = {}

        def opgroup(Q, a, dh):
            g = 2 * a + dh
            bk = g % 4
            b.wait("tensor", ve, zs_done[Q % 2])
            b.wait("tensor", ve, opevac[bk])
            b.wait("tensor", sc, exp_on_slot[bk // 2])
            op_tn = None
            for p in range(8):
                fn = lambda e, a=a, dh=dh, p=p, bk=bk, Q=Q: e.matmul(
                    OPBANKS[bk], zs[Q % 2][:, 512 * p + 128 * a:512 * p + 128 * a + 128],
                    wosb[:, 1024 * p + 512 * dh:1024 * p + 512 * dh + 512],
                    start=(p == 0), stop=(p == 7), skip_group_check=True)
                if p == 7:
                    op_tn = b.op("tensor", fn, inc=tn)
                else:
                    b.op("tensor", fn)
            zs_war[Q % 2] = op_tn
            b.wait("vector", tn, op_tn)
            b.wait("vector", ve, bo_bc_ve)
            b.wait("vector", ve, x1add_ve[(a - 1) % 2])
            opevac[bk] = b.op("vector", lambda e, dh=dh, bk=bk: e.tensor_add(
                attn_tmp[:, 512 * dh:512 * dh + 512], OPBANKS[bk],
                bo_bc[:, 512 * dh:512 * dh + 512]), inc=ve)

        def opchain(Q, a):
            ln.emit(attn_tmp[:], attn_tmp[:],
                    pre_vec_waits=[(ve, opevac[(2 * a) % 4]), (ve, opevac[(2 * a + 1) % 4])])
            sl = a % 2
            b.wait("vector", s_xs[sl], xs_cnt[(Q, a)])
            b.vspacer()
            x1add_ve[sl] = b.op("vector", lambda e, sl=sl: e.tensor_add(
                x1st[sl][:], x1st[sl][:], attn_tmp[:]), inc=ve)
            b.wait("sync", ve, x1add_ve[sl])
            x1st_war[sl] = b.op("sync", lambda e, Q=Q, a=a, sl=sl: e.dma_start(
                out=x1_h[4 * Q + a], in_=x1st[sl][:]), inc=s_x1[sl], k=16)

        qt_cnts = {}
        b.wait("sync", s_qst[0], qst_war[0])
        b.wait("sync", s_qst[1], qst_war[1])
        b.op("sync", lambda e: e.dma_start(out=qtr[0][:], in_=qt_h[0]), inc=s_qt[0], k=16)
        qt_cnts[0] = s_qt[0].n

        for qb in range(NBLK + 1):
            if qb + 1 < NBLK:
                sl1 = (qb + 1) % 2
                if qb >= 1:
                    b.wait("sync", tn, qld_war[sl1])
                b.op("sync", lambda e, qb=qb, sl1=sl1: e.dma_start(
                    out=qtr[sl1][:], in_=qt_h[qb + 1]), inc=s_qt[sl1], k=16)
                qt_cnts[qb + 1] = s_qt[sl1].n
            qt_cnt = qt_cnts.get(qb, 0)

            Q = qb - 1
            pieces = [[] for _ in range(8)]
            if Q >= 0:
                def mkx(a, Q=Q):
                    def f():
                        xs_cnt[(Q, a)] = xload(Q, a)
                    return f

                def mkg(a, dh, Q=Q):
                    return lambda: opgroup(Q, a, dh)

                def mkc(a, Q=Q):
                    return lambda: opchain(Q, a)

                pieces[0] = [mkx(0)]
                pieces[1] = [mkg(0, 0), mkx(1)]
                pieces[2] = [mkg(0, 1), mkc(0)]
                pieces[3] = [mkg(1, 0), mkx(2)]
                pieces[4] = [mkg(1, 1), mkc(1)]
                pieces[5] = [mkg(2, 0), mkx(3)]
                pieces[6] = [mkg(2, 1), mkc(2)]
                pieces[7] = [mkg(3, 0), mkg(3, 1), mkc(3)]

            if qb == NBLK:
                for ps in pieces:
                    for f in ps:
                        f()
                break

            nch = 4 * (qb + 1)
            for pj in range(8):
                for f in pieces[pj]:
                    f()
                p2i = pj % 2

                def scores(c, pj=pj, qb=qb, qt_cnt=qt_cnt):
                    s2 = c % 2
                    off = 128 * max(0, c - 4 * qb)
                    b.wait("tensor", sc, exp_on_slot[s2])
                    b.wait("tensor", ve, opevac[2 * s2])
                    b.wait("tensor", ve, opevac[2 * s2 + 1])
                    b.wait("tensor", s_qt[qb % 2], qt_cnt)
                    b.op("tensor", lambda e, c=c, pj=pj, s2=s2, qb=qb, off=off: e.matmul(
                        SCp[s2][:, off:512], kT[0:64, S * pj + 128 * c:S * pj + 128 * c + 128],
                        qtr[qb % 2][0:64, 512 * pj + off:512 * pj + 512],
                        start=True, stop=True, tile_position=(0, 0), skip_group_check=True))
                    return b.op("tensor", lambda e, c=c, pj=pj, s2=s2, qb=qb, off=off: e.matmul(
                        SCp[s2][:, 512 + off:1024], kT[64:128, S * pj + 128 * c:S * pj + 128 * c + 128],
                        qtr[qb % 2][64:128, 512 * pj + off:512 * pj + 512],
                        start=True, stop=True, tile_position=(64, 0), skip_group_check=True),
                        inc=tn)

                sc_tn = {0: scores(0)}
                if nch > 1:
                    sc_tn[1] = scores(1)
                if pend_fin is not None:
                    pend_fin()
                    pend_fin = None
                for c in range(nch):
                    s2 = c % 2
                    jj = c - 4 * qb
                    off = 128 * max(0, jj)
                    b.wait("scalar", tn, sc_tn[c])
                    b.wait("scalar", tn, expT_war[c % 2])
                    exp_sc = b.op("scalar", lambda e, c=c, s2=s2, off=off: e.activation(
                        expT[c % 2][:, off:1024], SCp[s2][:, off:1024], AF.Exp), inc=sc)
                    exp_on_slot[s2] = exp_sc
                    mask_ve = None
                    if jj >= 0:
                        b.wait("vector", sc, exp_sc)
                        for half in (0, 512):
                            mask_ve = b.op("vector", lambda e, c=c, h=half, o=off: e.tensor_mul(
                                expT[c % 2][:, h + o:h + o + 128],
                                expT[c % 2][:, h + o:h + o + 128],
                                tri[:, :]), inc=ve)
                    if c + 2 < nch:
                        sc_tn[c + 2] = scores(c + 2)
                    b.wait("tensor", sc, exp_sc)
                    if mask_ve is not None:
                        b.wait("tensor", ve, mask_ve)
                    st_ = c == 0
                    sp_ = c == nch - 1
                    if st_:
                        b.wait("tensor", ve, bo_bc_ve)
                        idx2 = qb * 8 + pj - 2
                        if idx2 >= 0:
                            b.wait("tensor", ve, mul_ve[idx2])
                    b.op("tensor", lambda e, c=c, pj=pj, st_=st_, sp_=sp_, p2i=p2i, off=off: e.matmul(
                        PZA[p2i][0:64, off:512], v_res[:, 1024 * c + 128 * pj:1024 * c + 128 * pj + 64],
                        expT[c % 2][:, off:512], start=st_, stop=sp_,
                        tile_position=(0, 0), skip_group_check=True))
                    b.op("tensor", lambda e, c=c, pj=pj, st_=st_, sp_=sp_, p2i=p2i, off=off: e.matmul(
                        PZA[p2i][64:128, off:512], v_res[:, 1024 * c + 128 * pj + 64:1024 * c + 128 * pj + 128],
                        expT[c % 2][:, 512 + off:1024], start=st_, stop=sp_,
                        tile_position=(0, 64), skip_group_check=True))
                    b.op("tensor", lambda e, c=c, pj=pj, st_=st_, sp_=sp_, p2i=p2i, off=off: e.matmul(
                        PZB[p2i][64:65, off:512], onescol[:, 0:1],
                        expT[c % 2][:, off:512], start=st_, stop=sp_,
                        tile_position=(0, 64), skip_group_check=True))
                    expT_war[c % 2] = b.op("tensor", lambda e, c=c, st_=st_, sp_=sp_, p2i=p2i, off=off: e.matmul(
                        PZB[p2i][32:33, off:512], onescol[:, 0:1],
                        expT[c % 2][:, 512 + off:1024], start=st_, stop=sp_,
                        tile_position=(0, 32), skip_group_check=True), inc=tn)
                zl_last = expT_war[(nch - 1) % 2]

                b.wait("vector", tn, zl_last)
                b.wait("vector", tn, bcast_tn)
                b.op("vector", lambda e, p2i=p2i: e.tensor_copy(lrows[64:65, :], PZB[p2i][64:65, :]))
                cp_sc = b.op("vector", lambda e, p2i=p2i: e.tensor_copy(
                    lrows[32:33, :], PZB[p2i][32:33, :]), inc=ve)

                def fin(qb=qb, pj=pj, p2i=p2i, cp_sc=cp_sc):
                    nonlocal bcast_tn
                    b.wait("tensor", ve, cp_sc)
                    b.op("tensor", lambda e, p2i=p2i: e.matmul(
                        PZB[p2i][0:64, :], onesf[64:65, 0:64], lrows[64:65, :],
                        start=True, stop=True, tile_position=(64, 0), skip_group_check=True))
                    bcast_tn = b.op("tensor", lambda e, p2i=p2i: e.matmul(
                        PZB[p2i][64:128, :], onesf[32:33, 0:64], lrows[32:33, :],
                        start=True, stop=True, tile_position=(32, 64), skip_group_check=True),
                        inc=tn)
                    b.wait("vector", tn, bcast_tn)
                    b.op("vector", lambda e, p2i=p2i: e.reciprocal(recipb[:], PZB[p2i][:, :]))
                    b.vspacer()
                    b.wait("vector", tn, zs_war[qb % 2])
                    mul_ve[qb * 8 + pj] = b.op("vector", lambda e, pj=pj, qb=qb, p2i=p2i:
                                               e.tensor_mul(
                                                   zs[qb % 2][:, 512 * pj:512 * pj + 512],
                                                   PZA[p2i][:, :], recipb[:]), inc=ve)

                pend_fin = fin
            pend_fin()
            pend_fin = None
            zs_done[qb % 2] = mul_ve[qb * 8 + 7]
            qld_war[qb % 2] = tn.n

        p2ps.close()
        p2.close()
        kv_stack.close()
        b.barrier()
        ffn = contextlib.ExitStack()
        lnS = [ffn.enter_context(nc.sbuf_tensor(f"lnS{i}", [128, 4096], BF16))
               for i in range(2)]
        w1sb = ffn.enter_context(nc.sbuf_tensor("w1sb", [128, 16384], BF16))
        w2sb = ffn.enter_context(nc.sbuf_tensor("w2sb", [128, 16384], BF16))
        hT = [ffn.enter_context(nc.sbuf_tensor(f"hT{i}", [128, 8192], BF16)) for i in range(2)]
        acc2 = [ffn.enter_context(nc.sbuf_tensor(f"acc2{i}", [128, D], F32)) for i in range(2)]
        ln2r = [ffn.enter_context(nc.sbuf_tensor(f"ln2r{i}", [128, D], BF16)) for i in range(2)]
        xr3 = [ffn.enter_context(nc.sbuf_tensor(f"xr3{i}", [128, D], F32)) for i in range(2)]
        xr4 = [ffn.enter_context(nc.sbuf_tensor(f"xr4{i}", [128, D], F32)) for i in range(2)]
        outst = [ffn.enter_context(nc.sbuf_tensor(f"outst{i}", [128, D], F32)) for i in range(2)]
        c2_bc = ffn.enter_context(nc.sbuf_tensor("c2_bc", [128, D], F32))
        fps = contextlib.ExitStack()
        T0f = fps.enter_context(nc.psum_tensor("T0f", [128, 1024], BF16))
        PZBf = fps.enter_context(nc.psum_tensor("PZBf", [128, 512], F32))

        cb_t = b.op("tensor", lambda e: e.matmul(
            PZBf[:, :], onesrowb[0:1, :], c2rowb[0:1, 0:512],
            start=True, stop=True), inc=tn)
        b.wait("vector", tn, cb_t)
        cb_v = b.op("vector", lambda e: e.tensor_copy(c2_bc[:, 0:512], PZBf[:, :]), inc=ve)
        b.wait("tensor", ve, cb_v)
        cb_t = b.op("tensor", lambda e: e.matmul(
            PZBf[:, :], onesrowb[0:1, :], c2rowb[0:1, 512:1024],
            start=True, stop=True), inc=tn)
        b.wait("vector", tn, cb_t)
        c2_bc_ve = b.op("vector", lambda e: e.tensor_copy(c2_bc[:, 512:1024], PZBf[:, :]),
                        inc=ve)
        b.wait("tensor", ve, c2_bc_ve)

        w2v = w2b_h.rearrange("E (k g p) n -> E k p g n", g=2, p=128)
        b.wait("sync", s_gp, precast_gp)
        for j in range(8):
            b.op("sync", lambda e, j=j: e.dma_start(
                out=w1sb[:, 2048 * j:2048 * j + 2048],
                in_=w1b_h[0][128 * j:128 * j + 128, :]), inc=s_w1, k=16)
        w1_cnt = s_w1.n
        for j in range(8):
            dst = w2sb.ap()[:, 2048 * j:2048 * j + 2048].rearrange("p (g n) -> p g n", g=2)
            b.op("sync", lambda e, j=j, d=dst: e.dma_start(
                out=d, in_=w2v[0, j]), inc=s_w2, k=16)
        w2_cnt = s_w2.n

        NS = 8
        YB = [SC0, SC1]
        HB = [PZA[0][:, :], PZA[1][:, :], PZBf[:, :]]
        x1_full = [s_x1[0].n, s_x1[1].n]
        a_norm = {}
        a_tr = {}
        a_done = [0, 0]
        b_done = {}
        hT_war = [0, 0]
        hrot_sc = [0] * 3
        yevac_ve = [0, 0]
        acc_war = [0, 0]
        out_war = [0, 0]
        xr3_war = [0, 0]
        xr4_war = [0, 0]
        t0_ve = 0
        tr_tn = 0

        def a_ln(s, tt):
            t = 4 * s + tt
            sl = t % 2
            b.wait("sync", s_x1[0], x1_full[0])
            b.wait("sync", s_x1[1], x1_full[1])
            b.wait("sync", ve, xr4_war[sl])
            b.op("sync", lambda e, t=t, sl=sl: e.dma_start(
                out=xr4[sl][:], in_=x1_h[t]), inc=s_x[sl], k=16)
            waits = [(s_x[sl], s_x[sl].n)]
            if t >= 2:
                waits.append((tn, a_tr.get(t - 2, 0)))
            a_norm[t] = ln.emit(xr4[sl][:], ln2r[sl][:], pre_vec_waits=waits)
            xr4_war[sl] = a_norm[t]

        def a_te(s, tt):
            nonlocal t0_ve, tr_tn
            t = 4 * s + tt
            sl = t % 2
            ls = s % 2
            b.wait("tensor", ve, a_norm[t])
            b.wait("tensor", ve, t0_ve)
            b.wait("tensor", tn, b_done.get(s - 2, 0))
            for j in range(8):
                fn = lambda e, j=j, sl=sl: e.transpose(
                    T0f[:, 128 * j:128 * j + 128], ln2r[sl][:, 128 * j:128 * j + 128],
                    ident_b[:, :])
                if j == 7:
                    tr_tn = b.op("tensor", fn, inc=tn)
                else:
                    b.op("tensor", fn)
            a_tr[t] = tr_tn
            b.wait("vector", tn, tr_tn)
            b.wait("vector", tn, b_done.get(s - 2, 0))
            src3 = T0f.ap().rearrange("p (j c) -> p j c", j=8)
            dst3 = lnS[ls].ap().rearrange("p (j s) -> p j s", j=8)[:, :, 128 * tt:128 * tt + 128]
            t0_ve = b.op("vector", lambda e, s3=src3, d3=dst3: e.tensor_copy(d3, s3), inc=ve)
            if tt == 3:
                a_done[ls] = t0_ve

        def stage_a(s):
            a_ln(s, 0)
            a_ln(s, 1)
            for tt in range(4):
                a_te(s, tt)
                if tt + 2 < 4:
                    a_ln(s, tt + 2)

        def stage_b(s):
            ls = s % 2
            hs = s % 2
            b.wait("tensor", ve, a_done[ls])
            b.wait("tensor", s_w1, w1_cnt)
            h_last = 0
            for m in range(16):
                hb = m % 3
                b.wait("tensor", sc, hrot_sc[hb])
                h_tn = None
                for k in range(8):
                    fn = lambda e, m=m, k=k, hb=hb, ls=ls: e.matmul(
                        HB[hb], w1sb[:, 2048 * k + 128 * m:2048 * k + 128 * m + 128],
                        lnS[ls][:, 512 * k:512 * k + 512],
                        start=(k == 0), stop=(k == 7))
                    if k == 7:
                        h_tn = b.op("tensor", fn, inc=tn)
                    else:
                        b.op("tensor", fn)
                h_last = h_tn
                b.wait("scalar", tn, h_tn)
                b.wait("scalar", tn, hT_war[hs])
                hrot_sc[hb] = b.op("scalar", lambda e, m=m, hs=hs, hb=hb: e.activation(
                    hT[hs][:, 512 * m:512 * m + 512], HB[hb], AF.Relu,
                    bias=b1g[:, m:m + 1], scale=GV[0]), inc=sc)
            b_done[s] = h_last
            return sc.n

        def stage_c(s, relu_done):
            hs = s % 2
            b.wait("tensor", sc, relu_done)
            b.wait("tensor", s_w2, w2_cnt)
            y_last = 0
            for a in range(4):
                t = 4 * s + a
                asl = a % 2
                yb = a % 2
                b.wait("tensor", ve, yevac_ve[yb])
                y_tn = None
                for m in range(16):
                    for dh in range(2):
                        fn = lambda e, m=m, dh=dh, a=a, hs=hs, yb=yb: e.matmul(
                            YB[yb][:, 512 * dh:512 * dh + 512],
                            hT[hs][:, 512 * m + 128 * a:512 * m + 128 * a + 128],
                            w2sb[:, 1024 * m + 512 * dh:1024 * m + 512 * dh + 512],
                            start=(m == 0), stop=(m == 15))
                        if m == 15 and dh == 1:
                            y_tn = b.op("tensor", fn, inc=tn)
                        else:
                            b.op("tensor", fn)
                y_last = y_tn
                b.wait("vector", tn, y_tn)
                b.wait("vector", ve, acc_war[asl])
                yevac_ve[yb] = b.op("vector", lambda e, yb=yb, asl=asl:
                                    e.scalar_tensor_tensor(
                                        acc2[asl][:], YB[yb][:, 0:1024], 1.0, c2_bc[:],
                                        ALU.mult, ALU.add), inc=ve)
                ln.emit(acc2[asl][:], acc2[asl][:],
                        pre_vec_waits=[(ve, yevac_ve[yb])])
                b.wait("sync", ve, xr3_war[asl])
                b.op("sync", lambda e, t=t, asl=asl: e.dma_start(
                    out=xr3[asl][:], in_=x1_h[t]), inc=s_xs[asl], k=16)
                b.wait("vector", s_xs[asl], s_xs[asl].n)
                b.wait("vector", s_out[asl], out_war[asl])
                b.vspacer()
                o_ve = b.op("vector", lambda e, asl=asl: e.tensor_add(
                    outst[asl][:], xr3[asl][:], acc2[asl][:]), inc=ve)
                xr3_war[asl] = o_ve
                acc_war[asl] = o_ve
                b.wait("gpsimd", ve, o_ve)
                out_war[asl] = b.op("gpsimd", lambda e, t=t, asl=asl: e.dma_start(
                    out=out_h[128 * t:128 * t + 128, :], in_=outst[asl][:]),
                    inc=s_out[asl], k=16)
            hT_war[hs] = y_last

        stage_a(0)
        stage_a(1)
        for s in range(NS):
            rd = stage_b(s)
            if s + 2 < NS:
                stage_a(s + 2)
            stage_c(s, rd)

        b.wait("gpsimd", s_out[0], s_out[0].n)
        b.wait("gpsimd", s_out[1], s_out[1].n)

        fps.close()
        ffn.close()
        pp.close()

        block = outer.enter_context(nc.Block())
        b.replay(block)
    return nc


def kernel(**inputs):
    x = np.ascontiguousarray(np.asarray(inputs["L0_attn_resid"], dtype=np.float32))
    wq = np.ascontiguousarray(np.asarray(inputs["L0_attn_W_Q"], np.float32).reshape(D, D))
    wk = np.ascontiguousarray(np.asarray(inputs["L0_attn_W_K"], np.float32).reshape(D, D))
    wv = np.ascontiguousarray(np.asarray(inputs["L0_attn_W_V"], np.float32).reshape(D, D))
    wo = np.ascontiguousarray(
        np.asarray(inputs["L0_attn_W_out"], np.float32).transpose(1, 0, 2).reshape(D, D))
    bo = np.ascontiguousarray(np.asarray(inputs["L0_attn_b_out"], np.float32).reshape(1, D))
    w1 = np.ascontiguousarray(np.asarray(inputs["L0_ffn_W1"], np.float32)[EXPERTS])
    w2 = np.ascontiguousarray(np.asarray(inputs["L0_ffn_W2"], np.float32)[EXPERTS])
    b1 = np.ascontiguousarray(
        np.asarray(inputs["L0_ffn_b1"], np.float32)[EXPERTS].reshape(NE * 16, 128))
    b2 = np.ascontiguousarray(np.asarray(inputs["L0_ffn_b2"], np.float32)[EXPERTS])

    gcol = _gates_f32().reshape(NE, 1)
    identb = np.eye(128, dtype=ml_dtypes.bfloat16)
    identf = np.eye(128, dtype=np.float32)
    onescol = np.ones((128, 1), ml_dtypes.bfloat16)
    onesrow = np.ones((1, 128), np.float32)
    onesf = np.ones((128, 64), np.float32)
    tri = np.triu(np.ones((128, 128), np.float32)).astype(ml_dtypes.bfloat16)

    nc = _build_program()
    shared = {"wq": wq, "wk": wk, "wv": wv, "wo": wo, "bo": bo, "w1": w1, "w2": w2,
              "b1": b1, "b2": b2, "gcol": gcol, "identb": identb, "identf": identf,
              "onescol": onescol, "onesrow": onesrow, "onesf": onesf, "tri": tri,
              "onesrowb": np.ones((1, 128), ml_dtypes.bfloat16)}
    in_maps = [dict(shared, x=x[i]) for i in range(8)]
    res = run_bass_kernel_spmd(nc, in_maps, core_ids=list(range(8)))
    return np.stack([np.asarray(r["out"], np.float32) for r in res.results], axis=0)
